# version 03 (final)
# baseline (speedup 1.0000x reference)
import math

import numpy as np

import concourse.bass as bass
import concourse.mybir as mybir
from concourse.bass_utils import run_bass_kernel_spmd

F32 = mybir.dt.float32
BF16 = mybir.dt.bfloat16
U8 = mybir.dt.uint8
I32 = mybir.dt.int32
ALU = mybir.AluOpType
AF = mybir.ActivationFunctionType
AX = mybir.AxisListType

N_CORES = 8
T = 2048
D = 1024
DI = 2048
NST = 16
Tc = 512
NQ = 4
NEG = -30000.0

ENGS = ("sync", "vector", "scalar", "tensor", "gpsimd")


class Sched:

    def __init__(self):
        self.ops = {e: [] for e in ENGS}
        self.count = {e: 0 for e in ENGS}

    def op(self, eng, fn, deps=(), inc=1):
        dd = []
        for d in deps:
            if d is None:
                continue
            if isinstance(d, list):
                dd.extend(x for x in d if x is not None)
            else:
                dd.append(d)
        self.count[eng] += inc
        self.ops[eng].append((fn, dd, inc))
        return (eng, self.count[eng])

    def barrier(self):
        return [(e, self.count[e]) for e in ENGS if self.count[e] > 0]

    def replay(self, eng_name, e, sems):
        satisfied = {k: 0 for k in ENGS}
        emitted = 0
        for fn, deps, inc in self.ops[eng_name]:
            need = {}
            for (peng, pval) in deps:
                need[peng] = max(need.get(peng, 0), pval)
            for peng, pval in need.items():
                if peng == eng_name and pval > emitted:
                    raise RuntimeError(f"bad self-dep on {eng_name}")
                if satisfied[peng] < pval:
                    e.wait_ge(sems[peng], pval)
                    satisfied[peng] = pval
            fn(e).then_inc(sems[eng_name], inc)
            emitted += inc
        if self.count[eng_name] > 0:
            e.wait_ge(sems[eng_name], self.count[eng_name])


def _host_constants():
    sel16 = np.zeros((16, 16 * 128), np.float32)
    for n in range(16):
        sel16[n, n * 128:(n + 1) * 128] = 1.0
    sel8 = np.zeros((8, 8 * 128), np.float32)
    for n in range(8):
        sel8[n, n * 128:(n + 1) * 128] = 1.0
    mask8 = np.zeros((128, 128), np.uint8)
    for ip in range(16):
        mask8[ip * 8:ip * 8 + 8, ip * 8:ip * 8 + 8] = 1
    ident = np.eye(128, dtype=np.float32)
    pos = np.arange(8, dtype=np.float32)[:, None]
    div = np.exp(np.arange(0, D, 2, dtype=np.float32) * (-math.log(10000.0) / D))
    ang = pos * div
    pe = np.stack([np.sin(ang), np.cos(ang)], axis=-1).reshape(8, D)[::-1]
    return dict(c_sel16=sel16, c_sel8=sel8, c_mask8=mask8, c_ident=ident,
                c_pet=np.ascontiguousarray(pe.T.astype(np.float32)))


def build_program(debug=0):
    nc = bass.Bass("TRN2", target_bir_lowering=False)
    S = Sched()

    def din(name, shape, dt=F32):
        return nc.dram_tensor(name, shape, dt, kind="ExternalInput")

    xin = din("xc", [T, D])
    w_ln_w = din("ln_w", [1, D])
    w_ln_b = din("ln_b", [1, D])
    w_inproj = din("inproj", [2 * DI, D])
    w_conv = din("convw", [DI, 4])
    w_convb = din("convb", [1, DI])
    w_xproj = din("xprojw", [96, DI])
    w_dtw = din("dtw", [DI, 64])
    w_dtb = din("dtb", [1, DI])
    w_alog = din("alog", [DI, NST])
    w_dvec = din("dvec", [1, DI])
    w_outproj = din("outproj", [D, DI])
    w_gamma = din("gamma", [1, D])
    w_beta = din("beta", [1, D])
    w_wq = din("wq", [D, D])
    w_wv = din("wv", [D, D])
    w_wr = din("wr", [D, D])
    w_ub = din("ubias", [1, D])
    w_vb = din("vbias", [1, D])
    w_wout = din("wout", [D, D])
    w_bout = din("bout", [1, D])
    w_wg = din("wg", [D, 8])
    w_bg = din("bg", [1, 8])
    w_w1 = din("w1", [8, D, D])
    w_b1 = din("b1", [8, D])
    w_w2 = din("w2", [8, D, D])
    w_b2 = din("b2", [8, D])
    c_sel16 = din("c_sel16", [16, 16 * 128])
    c_sel8 = din("c_sel8", [8, 8 * 128])
    c_mask8 = din("c_mask8", [128, 128], U8)
    c_ident = din("c_ident", [128, 128])
    c_pet = din("c_pet", [D, 8])
    c_cid = din("cid", [1, 1], I32)

    yout = nc.dram_tensor("y", [D, T], F32, kind="ExternalOutput")
    dbg = {}
    if debug >= 1:
        dbg["x1"] = nc.dram_tensor("dbg_x1", [8, 128, T], F32,
                                   kind="ExternalOutput")
    if debug >= 2:
        dbg["x2"] = nc.dram_tensor("dbg_x2", [8 * 8 * 128, 256], F32,
                                   kind="ExternalOutput")
    if debug >= 3:
        dbg["x3"] = nc.dram_tensor("dbg_x3", [8, 128, T], F32,
                                   kind="ExternalOutput")

    s_inprojT = nc.dram_tensor("s_inprojT", [32, 8, 128, 128], BF16)
    s_outprojT = nc.dram_tensor("s_outprojT", [8, 16, 128, 128], BF16)
    s_wq = nc.dram_tensor("s_wq", [8, 128, D], BF16)
    s_wv = nc.dram_tensor("s_wv", [8, 128, D], BF16)
    s_wout = nc.dram_tensor("s_wout", [8, 128, D], BF16)
    s_w1 = nc.dram_tensor("s_w1", [8, 8, 128, D], BF16)
    s_w2 = nc.dram_tensor("s_w2", [8, 8, 128, D], BF16)
    x1d = nc.dram_tensor("x1d", [8, 128, T], F32)
    cin = nc.dram_tensor("cc_in", [8 * 8 * 128, 256], F32)
    cout = nc.dram_tensor("xg", [N_CORES * 8 * 8 * 128, 256], F32,
                          kind="ExternalInput")
    x2o = nc.dram_tensor("x2o", [8 * 8 * 128, 256], F32, kind="ExternalOutput")

    ctxs = []

    def sbuf(name, shape, dt):
        c = nc.sbuf_tensor(name, shape, dt)
        ctxs.append(c)
        return c.__enter__()

    t_identf = sbuf("t_identf", [128, 128], F32)
    t_identb = sbuf("t_identb", [128, 128], BF16)
    t_sel16 = sbuf("t_sel16", [16, 16 * 128], BF16)
    t_sel8 = sbuf("t_sel8", [8, 8 * 128], BF16)
    t_mask8 = sbuf("t_mask8", [128, 128], U8)
    t_negc = sbuf("t_negc", [128, 128], F32)
    t_onesf = sbuf("t_onesf", [1, 128], F32)
    t_onescol_b = sbuf("t_onescol_b", [128, 1], BF16)
    t_aneg = sbuf("t_aneg", [128, 16 * NST], F32)
    t_dcol = sbuf("t_dcol", [128, 16], F32)
    t_cbcol = sbuf("t_cbcol", [128, 16], F32)
    t_dtbcol = sbuf("t_dtbcol", [128, 16], F32)
    t_cw = sbuf("t_cw", [128, 64], F32)
    t_lnw = sbuf("t_lnw", [128, 8], F32)
    t_lnb = sbuf("t_lnb", [128, 8], F32)
    t_gam = sbuf("t_gam", [128, 8], F32)
    t_bet = sbuf("t_bet", [128, 8], F32)
    t_ubc = sbuf("t_ubc", [128, 8], F32)
    t_vbc = sbuf("t_vbc", [128, 8], F32)
    t_boutc = sbuf("t_boutc", [128, 8], F32)
    t_b1c = sbuf("t_b1c", [128, 64], F32)
    t_bgc = sbuf("t_bgc", [8, 1], F32)
    t_b2 = sbuf("t_b2", [8, D], BF16)
    t_wg = sbuf("t_wg", [128, 64], BF16)
    t_dtwT = sbuf("t_dtwT", [64, DI], BF16)
    t_xpwT = sbuf("t_xpwT", [128, 16 * 96], BF16)
    t_rp = sbuf("t_rp", [128, 8 * 8 * 8], BF16)
    t_cid = sbuf("t_cid", [1, 1], I32)
    t_state = sbuf("t_state", [128, 256], F32)
    t_cvals = sbuf("t_cvals", [128, 8], F32)

    A1 = sbuf("A1", [128, 8192], F32)
    A2 = sbuf("A2", [128, 8192], F32)
    A3 = sbuf("A3", [128, 36864], BF16)
    A4 = sbuf("A4", [128, 8192], BF16)
    ABC = sbuf("ABC", [128, 1024], F32)
    ROWS = sbuf("ROWS", [1, 4096], F32)
    S8 = sbuf("S8", [8, 1024], F32)
    S96 = sbuf("S96", [96, 512], BF16)
    SBC = sbuf("SBC", [16, 1024], BF16)
    HALO = sbuf("HALO", [128, 64], F32)

    B = []
    for i in range(8):
        c = nc.psum_tensor(f"bank{i}", [128, 512], F32)
        ctxs.append(c)
        B.append(c.__enter__())

    sems = {}
    for e in ENGS:
        c = nc.semaphore(f"sem_{e}")
        ctxs.append(c)
        sems[e] = c.__enter__()

    bank_ev = {i: None for i in range(8)}
    reg_c = nc.sync.alloc_register("reg_c")
    reg_g = [[nc.sync.alloc_register(f"reg_g{i}_{h}") for h in range(2)]
             for i in range(8)]

    def dma(eng, out, in_, deps=()):
        return S.op(eng, lambda e, o=out, i=in_: e.dma_start(out=o, in_=i),
                    deps=deps, inc=16)

    def cp(out, in_, deps=()):
        return S.op("vector", lambda e, o=out, i=in_: e.tensor_copy(out=o, in_=i),
                    deps=deps)

    def cps(out, in_, deps=()):
        return S.op("scalar", lambda e, o=out, i=in_: e.copy(out=o, in_=i),
                    deps=deps)

    def act(out, in_, func, bias=0.0, scale=1.0, accum=None, deps=()):
        def f(e, o=out, i=in_, fu=func, b=bias, s=scale, a=accum):
            if a is not None:
                return e.activation(out=o, in_=i, func=fu, bias=b, scale=s,
                                    accum_out=a)
            return e.activation(out=o, in_=i, func=fu, bias=b, scale=s)
        return S.op("scalar", f, deps=deps)

    def mm(out, lhsT, rhs, start, stop, deps=()):
        return S.op("tensor",
                    lambda e, o=out, l=lhsT, r=rhs, st=start, sp=stop:
                    e.matmul(out=o, lhsT=l, rhs=r, start=st, stop=sp),
                    deps=deps)

    def tp(out, in_, ident, deps=()):
        return S.op("tensor",
                    lambda e, o=out, i=in_, d=ident:
                    e.transpose(out=o, in_=i, identity=d), deps=deps)

    def tt(out, in0, in1, op, deps=()):
        return S.op("vector",
                    lambda e, o=out, a=in0, b=in1, p=op:
                    e.tensor_tensor(out=o, in0=a, in1=b, op=p), deps=deps)

    def ts2(out, in0, s1, op0, s2, op1, deps=()):
        return S.op("vector",
                    lambda e, o=out, i=in0, a=s1, b=s2, p0=op0, p1=op1:
                    e.tensor_scalar(out=o, in0=i, scalar1=a, scalar2=b,
                                    op0=p0, op1=p1), deps=deps)

    def ts1(out, in0, s1, op0, deps=()):
        return S.op("vector",
                    lambda e, o=out, i=in0, a=s1, p0=op0:
                    e.tensor_single_scalar(out=o, in_=i, scalar=a, op=p0),
                    deps=deps)

    def stt(out, in0, scalar, in1, op0, op1, deps=()):
        return S.op("vector",
                    lambda e, o=out, i0=in0, s=scalar, i1=in1, p0=op0, p1=op1:
                    e.scalar_tensor_tensor(out=o, in0=i0, scalar=s, in1=i1,
                                           op0=p0, op1=p1), deps=deps)

    def vscan(out, d0, d1, init, op0=ALU.mult, op1=ALU.add, deps=()):
        return S.op("vector",
                    lambda e, o=out, a=d0, b=d1, i=init, p0=op0, p1=op1:
                    e.tensor_tensor_scan(out=o, data0=a, data1=b, initial=i,
                                         op0=p0, op1=p1), deps=deps)

    def vred(out, in_, op, deps=()):
        return S.op("vector",
                    lambda e, o=out, i=in_, p=op:
                    e.tensor_reduce(out=o, in_=i, axis=AX.X, op=p), deps=deps)

    def vsel(out, mask, on_true, on_false, deps=()):
        return S.op("vector",
                    lambda e, o=out, m=mask, a=on_true, b=on_false:
                    e.select(out=o, mask=m, on_true=a, on_false=b), deps=deps)

    def mset(eng, ap_, val, deps=()):
        return S.op(eng, lambda e, a=ap_, v=val: e.memset(a, v), deps=deps)

    def recip(out, in_, deps=()):
        return S.op("vector",
                    lambda e, o=out, i=in_: e.reciprocal(out=o, in_=i),
                    deps=deps)

    ev_id = dma("sync", t_identf[:], c_ident.ap())
    ev_idb = cp(t_identb[:], t_identf[:], deps=[ev_id])
    e1 = dma("sync", A2[0:16, 0:2048], c_sel16.ap())
    ev_sel16 = cp(t_sel16[:], A2[0:16, 0:2048], deps=[e1])
    e1 = dma("sync", A2[0:8, 2048:3072], c_sel8.ap())
    ev_sel8 = cp(t_sel8[:], A2[0:8, 2048:3072], deps=[e1])
    dma("sync", t_mask8[:], c_mask8.ap())
    mset("vector", t_negc[:], NEG)
    ev_onesf = mset("vector", t_onesf[:], 1.0)
    ev_onescol_b = mset("vector", t_onescol_b[:], 1.0)
    ev_cid = dma("sync", t_cid[:], c_cid.ap())
    mset("vector", t_state[:], 0.0)

    ev_cv = []
    for j, v in enumerate((0.0, 1.0, 1e-5, -0.5, 1.0 / 32.0)):
        ev_cv.append(mset("vector", t_cvals[:, j:j + 1], v))
        nc.const_aps.register(F32, v, t_cvals[:, j:j + 1])
    e1 = dma("sync", A2[:, 3072:3328].rearrange("p (k n) -> p k n", k=16),
             w_alog.ap().rearrange("(k p) n -> p k n", p=128))
    e2 = act(A1[:, 0:256], A2[:, 3072:3328], AF.Exp, deps=[e1] + ev_cv)
    ts1(t_aneg[:], A1[:, 0:256], -1.0, ALU.mult, deps=[e2])

    def col_load(src_ap, dst, n):
        eL = dma("sync", A1[0:n, 256:384], src_ap, deps=[colw[0]])
        eT = tp(B[6][:, 0:n], A1[0:n, 256:384], t_identf[0:n, 0:n],
                deps=[eL, ev_id, bank_ev[6], colw[0]])
        eE = cps(dst, B[6][:, 0:n], deps=[eT])
        bank_ev[6] = eE
        colw[0] = eE
        return eE

    colw = [None]
    for (src, dst, n) in (
            (w_dvec.ap().rearrange("o (k p) -> (o k) p", p=128), t_dcol[:], 16),
            (w_convb.ap().rearrange("o (k p) -> (o k) p", p=128), t_cbcol[:], 16),
            (w_dtb.ap().rearrange("o (k p) -> (o k) p", p=128), t_dtbcol[:], 16),
            (w_ln_w.ap().rearrange("o (m p) -> (o m) p", p=128), t_lnw[:], 8),
            (w_ln_b.ap().rearrange("o (m p) -> (o m) p", p=128), t_lnb[:], 8),
            (w_gamma.ap().rearrange("o (m p) -> (o m) p", p=128), t_gam[:], 8),
            (w_beta.ap().rearrange("o (m p) -> (o m) p", p=128), t_bet[:], 8),
            (w_ub.ap().rearrange("o (m p) -> (o m) p", p=128), t_ubc[:], 8),
            (w_vb.ap().rearrange("o (m p) -> (o m) p", p=128), t_vbc[:], 8),
            (w_bout.ap().rearrange("o (m p) -> (o m) p", p=128), t_boutc[:], 8),
            (w_b1.ap().rearrange("e (c p) -> (e c) p", p=128), t_b1c[:], 64),
    ):
        col_load(src, dst, n)
    dma("sync", t_cw[:].rearrange("p (k j) -> p k j", k=16),
        w_conv.ap().rearrange("(k p) j -> p k j", p=128))
    dma("sync", t_bgc[:], w_bg.ap().rearrange("o e -> e o"))
    e1 = dma("sync", A2[0:8, 3328:4352], w_b2.ap())
    cp(t_b2[:], A2[0:8, 3328:4352], deps=[e1])
    e1 = dma("sync", A2[:, 4352:4416].rearrange("p (k e) -> p k e", k=8),
             w_wg.ap().rearrange("(k p) e -> p k e", p=128))
    cp(t_wg[:], A2[:, 4352:4416], deps=[e1])

    def f_regs(e):
        e.reg_load(reg_c, t_cid[0:1, 0:1])
        e.reg_alu(reg_g[0][0], reg_c, 262144, ALU.mult)
        for i in range(8):
            if i > 0:
                e.reg_alu(reg_g[i][0], reg_g[i - 1][0], 2097152, ALU.add)
            e.reg_alu(reg_g[i][1], reg_g[i][0], 128, ALU.add)
        return e.nop()
    S.op("sync", f_regs, deps=[ev_cid])

    consts_done = S.barrier()
    lw = [None]
    ltp = [None]

    def conv_plain(src_rows_ap, dst_rows_ap, nslab):
        for s in range(nslab):
            eL = dma("sync", A2[:, 0:1024],
                     src_rows_ap[s * 128:(s + 1) * 128, :],
                     deps=[lw[0]] + (consts_done if s == 0 else []))
            eC = cp(A4[:, 0:1024], A2[:, 0:1024], deps=[eL])
            eW = dma("scalar", dst_rows_ap[s * 128:(s + 1) * 128, :],
                     A4[:, 0:1024], deps=[eC])
            lw[0] = eW

    conv_plain(w_wq.ap(), s_wq.ap().rearrange("k p d -> (k p) d"), 8)
    conv_plain(w_wv.ap(), s_wv.ap().rearrange("k p d -> (k p) d"), 8)
    conv_plain(w_wout.ap(), s_wout.ap().rearrange("k p d -> (k p) d"), 8)
    conv_plain(w_w1.ap().rearrange("e a b -> (e a) b"),
               s_w1.ap().rearrange("e k p d -> (e k p) d"), 64)
    conv_plain(w_w2.ap().rearrange("e a b -> (e a) b"),
               s_w2.ap().rearrange("e k p d -> (e k p) d"), 64)

    PB = B[7][:].bitcast(BF16)

    def conv_T(src_ap, dst_tile, nslab, nblk):
        for s in range(nslab):
            eL = dma("sync", A2[:, 0:nblk * 128],
                     src_ap[s * 128:(s + 1) * 128, :], deps=[lw[0]])
            eC = cp(A4[:, 0:nblk * 128], A2[:, 0:nblk * 128], deps=[eL])
            lw[0] = eC
            for b in range(nblk):
                eT = tp(PB[:, 0:128], A4[:, b * 128:(b + 1) * 128],
                        t_identb[:], deps=[eC, ev_idb, ltp[0]])
                eE = cps(A4[:, nblk * 128:nblk * 128 + 128], PB[:, 0:128],
                         deps=[eT, ltp[0]])
                eW = dma("scalar", dst_tile[s, b],
                         A4[:, nblk * 128:nblk * 128 + 128], deps=[eE])
                ltp[0] = eW
                lw[0] = eW

    conv_T(w_inproj.ap(), s_inprojT.ap(), 32, 8)
    conv_T(w_outproj.ap(), s_outprojT.ap(), 8, 16)

    eL = dma("sync", A2[0:96, 0:2048], w_xproj.ap(), deps=[lw[0]])
    eC = cp(A4[0:96, 0:2048], A2[0:96, 0:2048], deps=[eL])
    lw[0] = eC
    for k in range(16):
        eT = tp(PB[:, 0:96], A4[0:96, k * 128:(k + 1) * 128],
                t_identb[0:96, 0:96], deps=[eC, ev_idb, ltp[0]])
        eE = cps(t_xpwT[:, k * 96:(k + 1) * 96], PB[:, 0:96], deps=[eT, ltp[0]])
        ltp[0] = eE
    eL = dma("sync", A2[:, 2048:3072].rearrange("p (k r) -> p k r", k=16),
             w_dtw.ap().rearrange("(k p) r -> p k r", p=128), deps=[lw[0]])
    eC = cp(A4[:, 2176:3200], A2[:, 2048:3072], deps=[eL])
    lw[0] = eC
    for k in range(16):
        eT = tp(PB[0:64, 0:128], A4[:, 2176 + k * 64:2176 + (k + 1) * 64],
                t_identb[:], deps=[eC, ev_idb, ltp[0]])
        eE = cps(t_dtwT[0:64, k * 128:(k + 1) * 128], PB[0:64, 0:128],
                 deps=[eT, ltp[0]])
        ltp[0] = eE

    eL = dma("sync", A2[:, 3072:3136].rearrange("p (k j) -> p k j", k=8),
             c_pet.ap().rearrange("(k p) j -> p k j", p=128))
    ev_pet = cp(A4[:, 3200:3264], A2[:, 3072:3136], deps=[eL])
    e_last = None
    for k in range(8):
        eL = dma("sync", A2[:, 0:1024], w_wr.ap()[k * 128:(k + 1) * 128, :],
                 deps=[lw[0]])
        eC = cp(A4[:, 0:1024], A2[:, 0:1024], deps=[eL])
        lw[0] = eC
        for nf in range(8):
            e_last = mm(B[6][:, nf * 8:(nf + 1) * 8],
                        A4[:, nf * 128:(nf + 1) * 128],
                        A4[:, 3200 + k * 8:3200 + (k + 1) * 8],
                        start=(k == 0), stop=(k == 7),
                        deps=[eC, ev_pet, bank_ev[6]])
    e_rfm = cps(A4[:, 3264:3328], B[6][:, 0:64], deps=[e_last])
    bank_ev[6] = e_rfm
    ev_z = mset("vector", t_rp[:], 0.0)
    for n in range(8):
        for i in range(8):
            cp(t_rp[:, (n * 8 + i) * 8:(n * 8 + i) * 8 + i + 1],
               A4[:, 3264 + n * 8 + 7 - i:3264 + n * 8 + 8],
               deps=[e_rfm, ev_z])

    prep_done = S.barrier()

    XQ, XA = 0, 4096
    FS_A, FS_B, FT = 0, 512, 1024
    X1Q = 4096
    STG = 0
    XSTG, CONVIN, CONVC, DTT, TMP = 0, 4096, 4620, 5132, 5644
    BCA, BCB, BCC = 6156, 6668, 7180
    X3 = 0
    XBm, Hm, Um, SZm, Ym, BCN = 0, 4096, 8192, 16384, 24576, 32768
    XF, HH = 0, 4096
    HAa, Qa, VTa, OFa, ASCR = 0, 8192, 16384, 24576, 32768
    WS = [0, 2048]
    SCR = 4096
    dA_s = [SCR, SCR + 512]
    DU_s = SCR + 1024
    duB_s = SCR + 1536
    HSC_s = SCR + 2048
    hc_s = [SCR + 2560, SCR + 3072]
    YT_s = SCR + 3584
    MREP = SCR
    SROWB, WROWB, T1a, T3a, POSF, SELF, SF2 = (
        0, 1024, 2048, 3072, 4096, 4608, 4736)
    A_QUS, A_QVS, A_EXPW, A_ATT = ASCR, ASCR + 128, ASCR + 256, ASCR + 384

    R_MU, R_VAR, R_RSTD, R_NMR, R_T1, R_V2, R_RSTD2, R_SROW = (
        0, 512, 1024, 1536, 2048, 2560, 3072, 3584)
    R_WROW = R_T1

    def bc_ap(n, which):
        if n < 8:
            return A2[:, XSTG + n * 512:XSTG + (n + 1) * 512]\
                .bitcast(BF16)[:, which * 512:(which + 1) * 512]
        if n < 12:
            off = XBm + (n - 8) * 1024 + which * 512
            return A3[:, off:off + 512]
        off = BCN + (n - 12) * 1024 + which * 512
        return A3[:, off:off + 512]

    def ln_stats(get_xb_m, w, double, deps, sq_scr, ro=0):
        d0 = list(deps) + [ev_onescol_b, bank_ev[4], bank_ev[5]]
        ev_sum = None
        for m in range(8):
            ev_sum = mm(B[4][0:1, 0:w], t_onescol_b[:], get_xb_m(m),
                        start=(m == 0), stop=(m == 7), deps=d0)
        ev_sq = None
        for m in range(8):
            esq = tt(sq_scr, get_xb_m(m), get_xb_m(m), ALU.mult,
                     deps=d0 + [ev_sq])
            ev_sq = mm(B[5][0:1, 0:w], t_onescol_b[:], sq_scr,
                       start=(m == 0), stop=(m == 7), deps=[esq])

        def R(off):
            return ROWS[0:1, off + ro:off + ro + w]

        e_mu = ts1(R(R_MU), B[4][0:1, 0:w], 1.0 / 1024.0, ALU.mult,
                   deps=[ev_sum])
        bank_ev[4] = e_mu
        e_t1 = tt(R(R_T1), R(R_MU), R(R_MU), ALU.mult, deps=[e_mu])
        e_var = stt(R(R_VAR), B[5][0:1, 0:w], 1.0 / 1024.0, R(R_T1),
                    ALU.mult, ALU.subtract, deps=[ev_sq, e_t1])
        bank_ev[5] = e_var
        e_ln = act(R(R_T1), R(R_VAR), AF.Ln, bias=1e-5, deps=[e_var])
        e_rstd = act(R(R_RSTD), R(R_T1), AF.Exp, scale=-0.5, deps=[e_ln])
        e_nmr = stt(R(R_NMR), R(R_MU), -1.0, R(R_RSTD), ALU.mult, ALU.mult,
                    deps=[e_rstd])
        out = dict(rstd=e_rstd, nmr=e_nmr)
        if double:
            e_v2 = tt(R(R_V2), R(R_VAR), R(R_RSTD), ALU.mult, deps=[e_rstd])
            e_v2 = tt(R(R_V2), R(R_V2), R(R_RSTD), ALU.mult, deps=[e_v2])
            e_ln2 = act(R(R_V2), R(R_V2), AF.Ln, bias=1e-5, deps=[e_v2])
            e_rstd2 = act(R(R_RSTD2), R(R_V2), AF.Exp, scale=-0.5,
                          deps=[e_ln2])
            e_srow = tt(R(R_SROW), R(R_RSTD), R(R_RSTD2), ALU.mult,
                        deps=[e_rstd2])
            e_wrow = stt(R(R_WROW), R(R_MU), -1.0, R(R_SROW), ALU.mult,
                         ALU.mult, deps=[e_srow])
            out.update(rstd2=e_rstd2, srow=e_srow, wrow=e_wrow)
        return out

    def bcast_row(row_off, w, dst_f32, deps):
        e1 = mm(B[2][:, 0:w], t_onesf[0:1, :], ROWS[0:1, row_off:row_off + w],
                start=True, stop=True,
                deps=list(deps) + [ev_onesf, bank_ev[2]])
        e2 = cps(dst_f32, B[2][:, 0:w], deps=[e1])
        bank_ev[2] = e2
        return e2

    ws_ev = [None, None]
    ev_halo = {k: None for k in range(16)}
    x1dma_prev = [None]
    qbar = prep_done
    for q in range(NQ):
        eX = dma("sync", A2[:, XSTG:XSTG + 4096]
                 .rearrange("p (tt d) -> p tt d", tt=4),
                 xin.ap()[q * Tc:(q + 1) * Tc, :]
                 .rearrange("(tt p) d -> p tt d", p=128), deps=qbar)
        ev_xq = []
        for tti in range(4):
            for m in range(8):
                eT = tp(B[6][:, 0:128],
                        A2[:, XSTG + tti * 1024 + m * 128:
                           XSTG + tti * 1024 + (m + 1) * 128],
                        t_identf[:], deps=[eX, ev_id, bank_ev[6]])
                eE = cps(A1[:, XQ + m * Tc + tti * 128:
                            XQ + m * Tc + (tti + 1) * 128],
                         B[6][:, 0:128], deps=[eT])
                bank_ev[6] = eE
                ev_xq.append(eE)
        eXB = cp(A3[:, XBm:XBm + 4096], A1[:, XQ:XQ + 4096], deps=ev_xq)

        rows = ln_stats(lambda m: A3[:, XBm + m * Tc:XBm + (m + 1) * Tc], Tc,
                        True, [eXB], A4[:, SCR:SCR + 512])
        eb_rstd = bcast_row(R_RSTD, Tc, A2[:, BCA:BCA + Tc], [rows["rstd"]])
        eb_nmr = bcast_row(R_NMR, Tc, A2[:, BCB:BCB + Tc], [rows["nmr"]])
        eb_rstd2 = bcast_row(R_RSTD2, Tc, A2[:, BCC:BCC + Tc], [rows["rstd2"]])

        ev_xa = []
        ev_h = []
        e_prev = None
        for m in range(8):
            xq_m = A1[:, XQ + m * Tc:XQ + (m + 1) * Tc]
            xa_m = A1[:, XA + m * Tc:XA + (m + 1) * Tc]
            e1 = tt(A2[:, TMP:TMP + Tc], xq_m, A2[:, BCA:BCA + Tc], ALU.mult,
                    deps=[eb_rstd, e_prev])
            e2 = tt(xa_m, A2[:, TMP:TMP + Tc], A2[:, BCB:BCB + Tc], ALU.add,
                    deps=[e1, eb_nmr])
            ev_xa.append(e2)
            e3 = tt(A2[:, TMP:TMP + Tc], xa_m, A2[:, BCC:BCC + Tc], ALU.mult,
                    deps=[e2, eb_rstd2])
            e4 = ts2(A3[:, Hm + m * Tc:Hm + (m + 1) * Tc], A2[:, TMP:TMP + Tc],
                     t_lnw[:, m:m + 1], ALU.mult, t_lnb[:, m:m + 1], ALU.add,
                     deps=[e3])
            ev_h.append(e4)
            e_prev = e4

        ev_u = {}
        ev_sz = {}
        conv_ev = None
        for ec in range(32):
            bk = ec % 2
            slab = A4[:, WS[bk]:WS[bk] + 1024]
            eS = dma("sync", slab.rearrange("p (k e) -> p k e", k=8),
                     s_inprojT.ap()[ec].rearrange("k p e -> p k e"),
                     deps=[ws_ev[bk]] + (qbar if ec < 2 else []))
            emm = None
            for k in range(8):
                emm = mm(B[bk][:, 0:Tc], slab[:, k * 128:(k + 1) * 128],
                         A3[:, Hm + k * Tc:Hm + (k + 1) * Tc],
                         start=(k == 0), stop=(k == 7),
                         deps=[eS, ev_h[k]] + ([bank_ev[bk]] if k == 0 else []))
            ws_ev[bk] = emm
            if ec < 16:
                k = ec
                if q == 0:
                    eh = mset("vector", A2[:, CONVIN:CONVIN + 3], 0.0,
                              deps=[conv_ev])
                else:
                    eh = cp(A2[:, CONVIN:CONVIN + 3], HALO[:, k * 4:k * 4 + 3],
                            deps=[conv_ev, ev_halo[k]])
                eE = cps(A2[:, CONVIN + 3:CONVIN + 3 + Tc], B[bk][:, 0:Tc],
                         deps=[emm, eh])
                bank_ev[bk] = eE
                ev_halo[k] = cp(HALO[:, k * 4:k * 4 + 3],
                                A2[:, CONVIN + Tc:CONVIN + 3 + Tc], deps=[eE])
                ec0 = ts1(A2[:, CONVC:CONVC + Tc], A2[:, CONVIN:CONVIN + Tc],
                          t_cw[:, k * 4:k * 4 + 1], ALU.mult,
                          deps=[eE, eh, ev_halo[k]])
                for j in range(1, 4):
                    ec0 = stt(A2[:, CONVC:CONVC + Tc],
                              A2[:, CONVIN + j:CONVIN + j + Tc],
                              t_cw[:, k * 4 + j:k * 4 + j + 1],
                              A2[:, CONVC:CONVC + Tc], ALU.mult, ALU.add,
                              deps=[ec0])
                conv_ev = ec0
                ev_u[k] = act(A3[:, Um + k * Tc:Um + (k + 1) * Tc],
                              A2[:, CONVC:CONVC + Tc], AF.Silu,
                              bias=t_cbcol[:, k:k + 1], deps=[ec0])
            else:
                k = ec - 16
                ev_sz[k] = act(A3[:, SZm + k * Tc:SZm + (k + 1) * Tc],
                               B[bk][:, 0:Tc], AF.Silu, deps=[emm])
                bank_ev[bk] = ev_sz[k]

        e_xa_mm = None
        e_xb_mm = None
        for k in range(16):
            e_xa_mm = mm(B[3][0:80, 0:Tc], t_xpwT[:, k * 96:k * 96 + 80],
                         A3[:, Um + k * Tc:Um + (k + 1) * Tc],
                         start=(k == 0), stop=(k == 15),
                         deps=[ev_u[k]] + ([bank_ev[3]] if k == 0 else []))
            e_xb_mm = mm(B[2][0:16, 0:Tc], t_xpwT[:, k * 96 + 80:k * 96 + 96],
                         A3[:, Um + k * Tc:Um + (k + 1) * Tc],
                         start=(k == 0), stop=(k == 15),
                         deps=[ev_u[k]] + ([bank_ev[2]] if k == 0 else []))
        e_dtraw = cps(S96[0:64, 0:Tc], B[3][0:64, 0:Tc], deps=[e_xa_mm])
        e_B = cps(SBC[:, 0:Tc], B[3][64:80, 0:Tc], deps=[e_xa_mm])
        e_C = cps(SBC[:, Tc:2 * Tc], B[2][0:16, 0:Tc], deps=[e_xb_mm])
        bank_ev[3] = e_B
        bank_ev[2] = e_C

        bc_ready = {}
        for n in range(16):
            for which, src in ((0, SBC[:, 0:Tc]), (1, SBC[:, Tc:2 * Tc])):
                e1 = mm(B[2][:, 0:Tc], t_sel16[:, n * 128:(n + 1) * 128], src,
                        start=True, stop=True,
                        deps=[e_B, e_C, ev_sel16, bank_ev[2]])
                e2 = cps(bc_ap(n, which), B[2][:, 0:Tc], deps=[e1])
                bank_ev[2] = e2
                bc_ready[(n, which)] = e2

        dA_ev = [None, None]
        hc_ev = [None, None]
        ev_y = {}
        idx = 0
        for k in range(16):
            e_dt1 = mm(B[3][:, 0:Tc], t_dtwT[0:64, k * 128:(k + 1) * 128],
                       S96[0:64, 0:Tc], start=True, stop=True,
                       deps=[e_dtraw, bank_ev[3]])
            e_e = act(A2[:, TMP:TMP + Tc], B[3][:, 0:Tc], AF.Exp,
                      bias=t_dtbcol[:, k:k + 1], deps=[e_dt1])
            bank_ev[3] = e_e
            e_dt = act(A2[:, DTT:DTT + Tc], A2[:, TMP:TMP + Tc], AF.Ln,
                       bias=1.0, deps=[e_e])
            e_du = tt(A4[:, DU_s:DU_s + Tc], A2[:, DTT:DTT + Tc],
                      A3[:, Um + k * Tc:Um + (k + 1) * Tc], ALU.mult,
                      deps=[e_dt, ev_u[k]])
            e_my = None
            for n in range(16):
                sl = idx % 2
                e_dA = act(A4[:, dA_s[sl]:dA_s[sl] + Tc], A2[:, DTT:DTT + Tc],
                           AF.Exp, scale=t_aneg[:, k * 16 + n:k * 16 + n + 1],
                           deps=[e_dt, dA_ev[sl]])
                e_duB = tt(A4[:, duB_s:duB_s + Tc], A4[:, DU_s:DU_s + Tc],
                           bc_ap(n, 0), ALU.mult,
                           deps=[e_du, bc_ready[(n, 0)]])
                e_sc = vscan(A4[:, HSC_s:HSC_s + Tc],
                             A4[:, dA_s[sl]:dA_s[sl] + Tc],
                             A4[:, duB_s:duB_s + Tc],
                             t_state[:, k * 16 + n:k * 16 + n + 1],
                             deps=[e_dA, e_duB])
                dA_ev[sl] = e_sc
                cp(t_state[:, k * 16 + n:k * 16 + n + 1],
                   A4[:, HSC_s + Tc - 1:HSC_s + Tc], deps=[e_sc])
                e_hc = tt(A4[:, hc_s[sl]:hc_s[sl] + Tc],
                          A4[:, HSC_s:HSC_s + Tc], bc_ap(n, 1), ALU.mult,
                          deps=[e_sc, bc_ready[(n, 1)], hc_ev[sl]])
                e_my = mm(B[4][:, 0:Tc], t_identb[:],
                          A4[:, hc_s[sl]:hc_s[sl] + Tc],
                          start=(n == 0), stop=(n == 15),
                          deps=[e_hc, ev_idb] +
                          ([bank_ev[4]] if n == 0 else []))
                hc_ev[sl] = e_my
                idx += 1
            e_y1 = stt(A4[:, YT_s:YT_s + Tc],
                       A3[:, Um + k * Tc:Um + (k + 1) * Tc],
                       t_dcol[:, k:k + 1], B[4][:, 0:Tc], ALU.mult, ALU.add,
                       deps=[e_my, ev_u[k]])
            bank_ev[4] = e_y1
            e_y2 = tt(A3[:, Ym + k * Tc:Ym + (k + 1) * Tc],
                      A4[:, YT_s:YT_s + Tc],
                      A3[:, SZm + k * Tc:SZm + (k + 1) * Tc], ALU.mult,
                      deps=[e_y1, ev_sz[k]])
            ev_y[k] = e_y2

        for m in range(8):
            bk = 5 if m % 2 == 0 else 6
            emm = None
            for half in range(2):
                slab = A4[:, WS[half]:WS[half] + 1024]
                eS = dma("sync", slab.rearrange("p (k e) -> p k e", k=8),
                         s_outprojT.ap()[m, half * 8:(half + 1) * 8]
                         .rearrange("k p e -> p k e"),
                         deps=[ws_ev[half]])
                for kk in range(8):
                    k = half * 8 + kk
                    emm = mm(B[bk][:, 0:Tc], slab[:, kk * 128:(kk + 1) * 128],
                             A3[:, Ym + k * Tc:Ym + (k + 1) * Tc],
                             start=(k == 0), stop=(k == 15),
                             deps=[eS, ev_y[k]] +
                             ([bank_ev[bk]] if k == 0 else []))
                ws_ev[half] = emm
            e1 = tt(A2[:, TMP:TMP + Tc], B[bk][:, 0:Tc],
                    A1[:, XQ + m * Tc:XQ + (m + 1) * Tc], ALU.add, deps=[emm])
            bank_ev[bk] = e1
            e2 = tt(A2[:, CONVC:CONVC + Tc], A2[:, TMP:TMP + Tc],
                    A1[:, XA + m * Tc:XA + (m + 1) * Tc], ALU.add,
                    deps=[e1, ev_xa[m], x1dma_prev[0]])
            x1dma_prev[0] = dma("scalar", x1d.ap()[m, :, q * Tc:(q + 1) * Tc],
                                A2[:, CONVC:CONVC + Tc], deps=[e2])
        qbar = S.barrier()

    mamba_done = S.barrier()
    if debug >= 1:
        dma("scalar", dbg["x1"].ap(), x1d.ap(), deps=mamba_done)

    S8bv = S8[0:8, 0:256].bitcast(BF16)
    ABCb = ABC[:, 256:512].bitcast(BF16)

    def ffn_quarter(get_x_m, write_out, bar):
        eXB2 = None
        for m in range(8):
            eXB2 = cp(A3[:, XF + m * Tc:XF + (m + 1) * Tc], get_x_m(m),
                      deps=bar + [eXB2])
        rows = ln_stats(lambda m: A3[:, XF + m * Tc:XF + (m + 1) * Tc], Tc,
                        False, [eXB2], A4[:, MREP:MREP + 512])
        eb_r = bcast_row(R_RSTD, Tc, A1[:, FS_A:FS_A + Tc], [rows["rstd"]])
        eb_n = bcast_row(R_NMR, Tc, A1[:, FS_B:FS_B + Tc], [rows["nmr"]])
        ev_xf = []
        e_prev = None
        for m in range(8):
            e1 = tt(A1[:, FT:FT + Tc], get_x_m(m), A1[:, FS_A:FS_A + Tc],
                    ALU.mult, deps=[eb_r, e_prev])
            e2 = tt(A3[:, XF + m * Tc:XF + (m + 1) * Tc], A1[:, FT:FT + Tc],
                    A1[:, FS_B:FS_B + Tc], ALU.add, deps=[e1, eb_n])
            ev_xf.append(e2)
            e_prev = e2

        e_lg = None
        for k in range(8):
            e_lg = mm(B[5][0:8, 0:Tc], t_wg[:, k * 8:(k + 1) * 8],
                      A3[:, XF + k * Tc:XF + (k + 1) * Tc],
                      start=(k == 0), stop=(k == 7),
                      deps=[ev_xf[k]] + ([bank_ev[5]] if k == 0 else []))
        e_lgs = ts1(S8[0:8, 0:Tc], B[5][0:8, 0:Tc], t_bgc[0:8, 0:1],
                    ALU.add, deps=[e_lg])
        bank_ev[5] = e_lgs
        e_zro = mset("vector", ABC[:, 128:136], 0.0, deps=[])
        e_oh = []
        for tti in range(4):
            eT = tp(B[6][:, 0:8], S8[0:8, tti * 128:(tti + 1) * 128],
                    t_identf[0:8, 0:8], deps=[e_lgs, bank_ev[6]])
            eE = cps(ABC[:, tti * 8:tti * 8 + 8], B[6][:, 0:8], deps=[eT])
            bank_ev[6] = eE
            e_mx = vred(ABC[:, 32 + tti:33 + tti], ABC[:, tti * 8:tti * 8 + 8],
                        ALU.max, deps=[eE])
            e_ge = ts1(ABC[:, 64 + tti * 8:72 + tti * 8],
                       ABC[:, tti * 8:tti * 8 + 8],
                       ABC[:, 32 + tti:33 + tti], ALU.is_ge, deps=[e_mx])
            e_cs = vscan(ABC[:, 144 + tti * 8:152 + tti * 8],
                         ABC[:, 64 + tti * 8:72 + tti * 8],
                         ABC[:, 128:136], 0.0, op0=ALU.add, op1=ALU.add,
                         deps=[e_ge, e_zro])
            e_le = ts1(ABC[:, 200 + tti * 8:208 + tti * 8],
                       ABC[:, 144 + tti * 8:152 + tti * 8], 1.0, ALU.is_le,
                       deps=[e_cs])
            e_ohm = tt(ABCb[:, tti * 8:tti * 8 + 8],
                       ABC[:, 64 + tti * 8:72 + tti * 8],
                       ABC[:, 200 + tti * 8:208 + tti * 8], ALU.mult,
                       deps=[e_le])
            e_oh.append(e_ohm)
        e_ohT = []
        for tti in range(4):
            eT = tp(PB[0:8, 0:128], ABCb[:, tti * 8:tti * 8 + 8],
                    t_identb[:], deps=[e_oh[tti], bank_ev[7]])
            eE = cps(S8bv[0:8, tti * 128:(tti + 1) * 128], PB[0:8, 0:128],
                     deps=[eT])
            bank_ev[7] = eE
            e_ohT.append(eE)

        ev_mr = {}
        for e_ in range(8):
            e1 = mm(B[2][:, 0:Tc], t_sel8[:, e_ * 128:(e_ + 1) * 128],
                    S8bv[0:8, 0:Tc], start=True, stop=True,
                    deps=e_ohT + [ev_sel8, bank_ev[2]])
            e2 = cps(A4[:, MREP + e_ * 512:MREP + (e_ + 1) * 512],
                     B[2][:, 0:Tc], deps=[e1])
            bank_ev[2] = e2
            ev_mr[e_] = e2

        ev_hh = {}
        for e_ in range(8):
            emms = {}
            for k in range(8):
                bks = k % 2
                slab = A4[:, WS[bks]:WS[bks] + 1024]
                eS = dma("sync", slab, s_w1.ap()[e_, k], deps=[ws_ev[bks]])
                for m in range(8):
                    emms[m] = mm(B[m][:, 0:Tc], slab[:, m * 128:(m + 1) * 128],
                                 A3[:, XF + k * Tc:XF + (k + 1) * Tc],
                                 start=(k == 0), stop=(k == 7),
                                 deps=[eS, ev_xf[k]] +
                                 ([bank_ev[m]] if k == 0 else []))
                ws_ev[bks] = emms[7]
            for m in range(8):
                hho = A3[:, HH + e_ * 4096 + m * Tc:HH + e_ * 4096 + (m + 1) * Tc]
                eG = act(hho, B[m][:, 0:Tc], AF.Gelu,
                         bias=t_b1c[:, e_ * 8 + m:e_ * 8 + m + 1],
                         deps=[emms[m]])
                bank_ev[m] = eG
                ev_hh[(e_, m)] = tt(hho, hho,
                                    A4[:, MREP + e_ * 512:MREP + (e_ + 1) * 512],
                                    ALU.mult, deps=[eG, ev_mr[e_]])

        emm2 = {m: None for m in range(8)}
        first = {m: True for m in range(8)}
        for e_ in range(8):
            for k in range(8):
                bks = k % 2
                slab = A4[:, WS[bks]:WS[bks] + 1024]
                eS = dma("sync", slab, s_w2.ap()[e_, k], deps=[ws_ev[bks]])
                for m in range(8):
                    emm2[m] = mm(B[m][:, 0:Tc], slab[:, m * 128:(m + 1) * 128],
                                 A3[:, HH + e_ * 4096 + k * Tc:
                                    HH + e_ * 4096 + (k + 1) * Tc],
                                 start=(e_ == 0 and k == 0), stop=False,
                                 deps=[eS, ev_hh[(e_, k)]] +
                                 ([bank_ev[m]] if first[m] else []))
                    first[m] = False
                ws_ev[bks] = emm2[7]
        ev_out = []
        for m in range(8):
            eB2 = mm(B[m][:, 0:Tc], t_b2[:, m * 128:(m + 1) * 128],
                     S8bv[0:8, 0:Tc], start=False, stop=True,
                     deps=[emm2[m]] + e_ohT)
            e1 = tt(A1[:, FT:FT + Tc], B[m][:, 0:Tc], get_x_m(m), ALU.add,
                    deps=[eB2, ev_out[-1] if ev_out else None])
            eW = write_out(m, A1[:, FT:FT + Tc], [e1])
            bank_ev[m] = eW
            ev_out.append(eW)
        return ev_out

    fbar = mamba_done
    for q in range(NQ):
        eXL = dma("sync", A1[:, X1Q:X1Q + 4096]
                  .rearrange("p (m t) -> p m t", m=8),
                  x1d.ap()[:, :, q * Tc:(q + 1) * Tc]
                  .rearrange("m p t -> p m t"), deps=fbar)

        def get_x1(m):
            return A1[:, X1Q + m * Tc:X1Q + (m + 1) * Tc]

        def wout1(m, src_ap, deps, q=q):
            e1 = cp(A2[:, X3 + m * Tc:X3 + (m + 1) * Tc]
                    .rearrange("p (d j) -> p d j", d=8),
                    src_ap.rearrange("p (j d) -> p d j", d=8), deps=deps)
            dma("scalar",
                cin.ap().rearrange("(d m p) t -> d m p t", d=8, m=8)
                [:, m, :, q * 64:(q + 1) * 64]
                .rearrange("d p j -> p d j"),
                A2[:, X3 + m * Tc:X3 + (m + 1) * Tc]
                .rearrange("p (d j) -> p d j", d=8), deps=[e1])
            return dma("scalar",
                       x2o.ap().rearrange("(d m p) t -> d m p t", d=8, m=8)
                       [:, m, :, q * 64:(q + 1) * 64]
                       .rearrange("d p j -> p d j"),
                       A2[:, X3 + m * Tc:X3 + (m + 1) * Tc]
                       .rearrange("p (d j) -> p d j", d=8), deps=[e1])

        ffn_quarter(get_x1, wout1, [eXL] + fbar)
        fbar = S.barrier()

    ffn1_done = S.barrier()

    e_ag = None

    abar = S.barrier()
    qvs_ev = [None]
    war_sm = [None]
    att_war = [None]
    for hf in range(2):
        eG = None
        for i in range(8):
            src = bass.AP(tensor=cout, offset=reg_g[i][hf],
                          ap=[[32768, 8], [256, 128], [1, 128]])
            eG = dma("sync",
                     A1[:, STG + i * 1024:STG + (i + 1) * 1024]
                     .rearrange("p (m ip) -> p m ip", m=8),
                     src.transpose([1, 0, 2]), deps=abar)
        eXB3 = cp(A3[:, Qa:Qa + 8192], A1[:, STG:STG + 8192], deps=[eG])

        def attn_xb(m, hh):
            return A3[:, Qa:Qa + 8192]\
                .rearrange("p (i m2 ip) -> p m2 i ip", m2=8, i=8)\
                [:, m, hh * 4:(hh + 1) * 4, :]

        rows0 = ln_stats(lambda m: attn_xb(m, 0), 512, True, [eXB3],
                         A4[:, SCR:SCR + 512], ro=0)
        eb_s1 = bcast_row(R_SROW, 512, A2[:, SROWB:SROWB + 512],
                          [rows0["srow"]])
        eb_w1 = bcast_row(R_WROW, 512, A2[:, WROWB:WROWB + 512],
                          [rows0["wrow"]])
        rows1 = ln_stats(lambda m: attn_xb(m, 1), 512, True,
                         [eXB3, eb_s1, eb_w1], A4[:, SCR:SCR + 512], ro=0)
        eb_s2 = bcast_row(R_SROW, 512, A2[:, SROWB + 512:SROWB + 1024],
                          [rows1["srow"]])
        eb_w2 = bcast_row(R_WROW, 512, A2[:, WROWB + 512:WROWB + 1024],
                          [rows1["wrow"]])

        ev_ha = []
        e_prev = None
        for m in range(8):
            src_m = A1[:, STG:STG + 8192]\
                .rearrange("p (i m2 ip) -> p m2 i ip", m2=8, i=8)[:, m]
            e1 = tt(A2[:, T1a:T1a + 1024]
                    .rearrange("p (i ip) -> p i ip", i=8),
                    src_m, A2[:, SROWB:SROWB + 1024]
                    .rearrange("p (i ip) -> p i ip", i=8),
                    ALU.mult, deps=[eb_s1, eb_s2, e_prev])
            e3 = ts2(A2[:, T3a:T3a + 1024], A2[:, WROWB:WROWB + 1024],
                     t_gam[:, m:m + 1], ALU.mult, t_bet[:, m:m + 1], ALU.add,
                     deps=[eb_w1, eb_w2, e_prev])
            e4 = stt(A3[:, HAa + m * 1024:HAa + (m + 1) * 1024]
                     .rearrange("p (ip i) -> p i ip", i=8),
                     A2[:, T1a:T1a + 1024]
                     .rearrange("p (i ip) -> p i ip", i=8),
                     t_gam[:, m:m + 1],
                     A2[:, T3a:T3a + 1024]
                     .rearrange("p (i ip) -> p i ip", i=8),
                     ALU.mult, ALU.add, deps=[e1, e3])
            ev_ha.append(e4)
            e_prev = e4

        def big_mm(wdram, out_base, evsrc):
            ev_ev = {}
            for tsub in range(2):
                emms = {}
                for k in range(8):
                    bks = k % 2
                    slab = A4[:, WS[bks]:WS[bks] + 1024]
                    eS = dma("sync", slab, wdram.ap()[k], deps=[ws_ev[bks]])
                    for m in range(8):
                        emms[m] = mm(B[m][:, 0:Tc],
                                     slab[:, m * 128:(m + 1) * 128],
                                     A3[:, HAa + k * 1024 + tsub * Tc:
                                        HAa + k * 1024 + (tsub + 1) * Tc],
                                     start=(k == 0), stop=(k == 7),
                                     deps=[eS, evsrc[k]] +
                                     ([bank_ev[m]] if k == 0 else []))
                    ws_ev[bks] = emms[7]
                for m in range(8):
                    ev_ev[(m, tsub)] = cps(
                        A3[:, out_base + m * 1024 + tsub * Tc:
                           out_base + m * 1024 + (tsub + 1) * Tc],
                        B[m][:, 0:Tc], deps=[emms[m]])
                    bank_ev[m] = ev_ev[(m, tsub)]
            return ev_ev

        ev_q = big_mm(s_wq, Qa, ev_ha)
        ev_v = big_mm(s_wv, OFa, ev_ha)
        ev_vt = {}
        for m in range(8):
            for tti in range(8):
                eT = tp(PB[:, 0:128],
                        A3[:, OFa + m * 1024 + tti * 128:
                           OFa + m * 1024 + (tti + 1) * 128],
                        t_identb[:],
                        deps=[ev_v[(m, tti // 4)], ev_idb, bank_ev[7]])
                eE = cps(A3[:, VTa + tti * 1024 + m * 128:
                            VTa + tti * 1024 + (m + 1) * 128],
                         PB[:, 0:128], deps=[eT])
                bank_ev[7] = eE
                ev_vt[(m, tti)] = eE

        ev_post = {}
        for n in range(8):
            e_pw = []
            for i in range(8):
                e_qv = ts1(A3[:, A_QVS:A_QVS + 128],
                           A3[:, Qa + n * 1024:Qa + (n + 1) * 1024]
                           .rearrange("p (ip i) -> p ip i", i=8)[:, :, i],
                           t_vbc[:, n:n + 1], ALU.add,
                           deps=[ev_q[(n, 0)], ev_q[(n, 1)], qvs_ev[0]])
                e_pm = mm(B[2][0:8, 0:128], t_rp[:, (n * 8 + i) * 8:
                                                (n * 8 + i) * 8 + 8],
                          A3[:, A_QVS:A_QVS + 128], start=True, stop=True,
                          deps=[e_qv, bank_ev[2]])
                e_pe = cps(S8[0:8, 0:1024]
                           .rearrange("q (ip i) -> q ip i", i=8)[:, :, i],
                           B[2][0:8, 0:128], deps=[e_pm])
                bank_ev[2] = e_pe
                qvs_ev[0] = e_pm
                e_pw.append(e_pe)
            for tti in range(8):
                eT = tp(B[6][:, 0:8], S8[0:8, tti * 128:(tti + 1) * 128],
                        t_identf[0:8, 0:8], deps=e_pw + [bank_ev[6]])
                eE = cps(A2[:, POSF + n * 64 + tti * 8:
                            POSF + n * 64 + (tti + 1) * 8],
                         B[6][:, 0:8], deps=[eT])
                bank_ev[6] = eE
                ev_post[(n, tti)] = eE

        ev_of = {}
        sc_ev = [None, None]
        pv_ev = [None, None]
        sel_ev = [None]
        qus_ev = [None]
        for tti in range(8):
            for n in range(8):
                sl = (tti * 8 + n) % 2
                e_qu = ts1(A3[:, A_QUS:A_QUS + 128],
                           A3[:, Qa + n * 1024 + tti * 128:
                              Qa + n * 1024 + (tti + 1) * 128],
                           t_ubc[:, n:n + 1], ALU.add,
                           deps=[ev_q[(n, tti // 4)], qus_ev[0]])
                e_sm = mm(B[sl][:, 0:128], A3[:, A_QUS:A_QUS + 128],
                          A3[:, Qa + n * 1024 + tti * 128:
                             Qa + n * 1024 + (tti + 1) * 128],
                          start=True, stop=True,
                          deps=[e_qu, sc_ev[sl]])
                qus_ev[0] = e_sm
                e_ps = vsel(A2[:, SELF:SELF + 128]
                            .rearrange("p (a b) -> p a b", a=16),
                            t_mask8[:].rearrange("p (a b) -> p a b", a=16),
                            A2[:, POSF + n * 64 + tti * 8:
                               POSF + n * 64 + tti * 8 + 8]
                            .unsqueeze(1).broadcast_to([128, 16, 8]),
                            t_negc[:].rearrange("p (a b) -> p a b", a=16),
                            deps=[ev_post[(n, tti)], sel_ev[0]])
                e_sf = tt(A2[:, SF2:SF2 + 128], B[sl][:, 0:128],
                          A2[:, SELF:SELF + 128], ALU.add,
                          deps=[e_sm, e_ps])
                sc_ev[sl] = e_sf
                sel_ev[0] = e_sf
                e_mx = vred(ABC[:, 0:1], A2[:, SF2:SF2 + 128], ALU.max,
                            deps=[e_sf])
                e_bc = ts1(ABC[:, 1:2], ABC[:, 0:1], -1.0 / 32.0, ALU.mult,
                           deps=[e_mx])
                e_ex = act(A3[:, A_EXPW:A_EXPW + 128], A2[:, SF2:SF2 + 128],
                           AF.Exp, bias=ABC[:, 1:2], scale=1.0 / 32.0,
                           accum=ABC[:, 2:3], deps=[e_bc, war_sm[0]])
                e_ri = recip(ABC[:, 3:4], ABC[:, 2:3], deps=[e_ex])
                e_aw = ts1(A3[:, A_ATT:A_ATT + 128], A3[:, A_EXPW:A_EXPW + 128],
                           ABC[:, 3:4], ALU.mult, deps=[e_ri])
                war_sm[0] = e_aw
                eT = tp(PB[:, 0:128], A3[:, A_ATT:A_ATT + 128], t_identb[:],
                        deps=[e_aw, bank_ev[7]])
                eTe = cps(A3[:, A_ATT + 128:A_ATT + 256], PB[:, 0:128],
                          deps=[eT, att_war[0]])
                bank_ev[7] = eTe
                psl = 2 + (tti * 8 + n) % 2
                e_pv = mm(B[psl][:, 0:128],
                          A3[:, VTa + tti * 1024 + n * 128:
                             VTa + tti * 1024 + (n + 1) * 128],
                          A3[:, A_ATT + 128:A_ATT + 256],
                          start=True, stop=True,
                          deps=[eTe, ev_vt[(n, tti)], pv_ev[psl - 2]])
                e_oe = cps(A3[:, OFa + n * 1024 + tti * 128:
                              OFa + n * 1024 + (tti + 1) * 128],
                           B[psl][:, 0:128], deps=[e_pv])
                pv_ev[psl - 2] = e_oe
                att_war[0] = e_pv
                ev_of[(n, tti)] = e_oe

        ofs = list(ev_of.values())
        for tsub in range(2):
            emms = {}
            for k in range(8):
                bks = k % 2
                slab = A4[:, WS[bks]:WS[bks] + 1024]
                eS = dma("sync", slab, s_wout.ap()[k], deps=[ws_ev[bks]])
                for m in range(8):
                    emms[m] = mm(B[m][:, 0:Tc], slab[:, m * 128:(m + 1) * 128],
                                 A3[:, OFa + k * 1024 + tsub * Tc:
                                    OFa + k * 1024 + (tsub + 1) * Tc],
                                 start=(k == 0), stop=(k == 7),
                                 deps=[eS] + (ofs if k == 0 else []) +
                                 ([bank_ev[m]] if k == 0 else []))
                ws_ev[bks] = emms[7]
            for m in range(8):
                x2r_v = A1[:, STG:STG + 8192]\
                    .rearrange("p (i m2 ip) -> p m2 ip i", m2=8, i=8)\
                    [:, m, tsub * 64:(tsub + 1) * 64, :]
                e1 = stt(A2[:, X3 + m * 1024 + tsub * Tc:
                            X3 + m * 1024 + (tsub + 1) * Tc]
                         .rearrange("p (ip i) -> p ip i", i=8),
                         B[m][:, 0:Tc].rearrange("p (ip i) -> p ip i", i=8),
                         t_boutc[:, m:m + 1], x2r_v, ALU.add, ALU.add,
                         deps=[emms[m]])
                bank_ev[m] = e1
        attn_done = S.barrier()
        if debug >= 3:
            for m in range(8):
                dma("scalar", dbg["x3"].ap()[m, :, hf * 1024:(hf + 1) * 1024],
                    A2[:, X3 + m * 1024:X3 + (m + 1) * 1024], deps=attn_done)

        qbar2 = attn_done
        for qq in range(2):
            def get_x3(m, qq=qq):
                return A2[:, X3 + m * 1024 + qq * Tc:X3 + m * 1024 + (qq + 1) * Tc]

            def wout2(m, src_ap, deps, qq=qq, hf=hf):
                col = hf * 1024 + qq * Tc
                return dma("scalar",
                           yout.ap()[m * 128:(m + 1) * 128, col:col + Tc],
                           src_ap, deps=deps)

            ffn_quarter(get_x3, wout2, qbar2)
            qbar2 = S.barrier()
        abar = S.barrier()

    with nc.Block() as block:
        @block.sync
        def _(e):
            S.replay("sync", e, sems)

        @block.vector
        def _(e):
            S.replay("vector", e, sems)

        @block.scalar
        def _(e):
            S.replay("scalar", e, sems)

        @block.tensor
        def _(e):
            S.replay("tensor", e, sems)

        @block.gpsimd
        def _(e):
            S.replay("gpsimd", e, sems)

    for c in reversed(ctxs):
        c.__exit__(None, None, None)
    return nc


def kernel(**inputs):
    consts = _host_constants()
    x = np.ascontiguousarray(np.asarray(inputs["L0_attn_a_x"], np.float32))

    def full(name):
        return np.ascontiguousarray(np.asarray(inputs[name], np.float32))

    base = {
        "ln_w": full("L0_attn_a_ln_w").reshape(1, D),
        "ln_b": full("L0_attn_a_ln_b").reshape(1, D),
        "inproj": full("L0_attn_a_in_proj_w"),
        "convw": full("L0_attn_a_conv_w").reshape(DI, 4),
        "convb": full("L0_attn_a_conv_b").reshape(1, DI),
        "xprojw": full("L0_attn_a_x_proj_w"),
        "dtw": full("L0_attn_a_dt_proj_w"),
        "dtb": full("L0_attn_a_dt_proj_b").reshape(1, DI),
        "alog": full("L0_attn_a_A_log"),
        "dvec": full("L0_attn_a_D").reshape(1, DI),
        "outproj": full("L0_attn_a_out_proj_w"),
        "gamma": full("L0_attn_b_gamma").reshape(1, D),
        "beta": full("L0_attn_b_beta").reshape(1, D),
        "wq": full("L0_attn_b_Wq"),
        "wv": full("L0_attn_b_Wv"),
        "wr": full("L0_attn_b_Wr"),
        "ubias": full("L0_attn_b_u_bias").reshape(1, D),
        "vbias": full("L0_attn_b_v_bias").reshape(1, D),
        "wout": full("L0_attn_b_Wout"),
        "bout": full("L0_attn_b_bout").reshape(1, D),
        "wg": full("L0_ffn_Wg"),
        "bg": full("L0_ffn_bg").reshape(1, 8),
        "w1": full("L0_ffn_W1"),
        "b1": full("L0_ffn_b1"),
        "w2": full("L0_ffn_W2"),
        "b2": full("L0_ffn_b2"),
    }
    base.update(consts)

    nc = build_program(debug=0)
    xg = np.zeros((N_CORES * 8 * 8 * 128, 256), np.float32)
    for launch in range(2):
        in_maps = []
        for c in range(N_CORES):
            m = dict(base)
            m["xc"] = x[c]
            m["cid"] = np.array([[c]], np.int32)
            m["xg"] = xg
            in_maps.append(m)
        res = run_bass_kernel_spmd(nc, in_maps, core_ids=list(range(N_CORES)))
        if launch == 0:
            xg = np.concatenate(
                [np.asarray(res.results[c]["x2o"]) for c in range(N_CORES)], 0)
            xg = np.ascontiguousarray(xg)

    out = np.empty((8, T, D), np.float32)
    for c in range(N_CORES):
        yc = np.asarray(res.results[c]["y"])
        ytok = np.ascontiguousarray(yc.T)
        toks = ytok.reshape(2, 128, 8, D)
        for hf in range(2):
            for i in range(8):
                p = c + 8 * (hf * 128 + np.arange(128))
                out[i, p, :] = toks[hf, :, i, :]
    return out
